# version 01 (final)
# baseline (speedup 1.0000x reference)
import numpy as np
import ml_dtypes
from contextlib import ExitStack

import concourse.bass as bass
import concourse.mybir as mybir
from concourse.bass_utils import run_bass_kernel_spmd

F32 = mybir.dt.float32
BF16 = mybir.dt.bfloat16
I32 = mybir.dt.int32
AF = mybir.ActivationFunctionType
ALU = mybir.AluOpType
AX = mybir.AxisListType

_STAGE = "full"
_CUT = 99

D = 1024
FF = 4096
SEQ = 2048
NB = 2
EPS = 1e-5
NTILE = 16
NBLK = 4
RLOW_OFF = 60.0
NTOT = float(SEQ * D)


class Plan:
    def __init__(self):
        self.ops = {k: [] for k in "tvsyg"}
        self.cnt = {}

    def op(self, eng, fn, incs=()):
        evs = []
        incs = list(incs)
        for (n, k) in incs:
            self.cnt[n] = self.cnt.get(n, 0) + k
            evs.append((n, self.cnt[n]))
        self.ops[eng].append(["op", fn, incs])
        return evs[0] if len(evs) == 1 else (evs or None)

    def mark(self, eng):
        for item in reversed(self.ops[eng]):
            if item[0] != "op":
                continue
            if any(k == 16 for (_n, k) in item[2]):
                continue
            for (n, _k) in item[2]:
                if n == eng:
                    return (n, self.cnt[n])
            self.cnt[eng] = self.cnt.get(eng, 0) + 1
            item[2].append((eng, 1))
            return (eng, self.cnt[eng])
        return self.op(eng, lambda e: e.nop(), [(eng, 1)])

    def w(self, eng, ev):
        if ev is None:
            return
        if isinstance(ev, list):
            for e in ev:
                self.w(eng, e)
            return
        self.ops[eng].append(("wait", ev))

    def t(self, fn, inc=False):
        return self.op("t", fn, [("t", 1)] if inc else ())

    def v(self, fn, inc=False):
        return self.op("v", fn, [("v", 1)] if inc else ())

    def s(self, fn, inc=False):
        return self.op("s", fn, [("s", 1)] if inc else ())

    def g(self, fn, inc=False):
        return self.op("g", fn, [("g", 1)] if inc else ())

    def dma(self, eng, out_fn, in_fn, slow=False, sem=None):
        if sem is None:
            sem = {"y": "dy", "s": "ds", "g": "dg"}[eng]
        if not hasattr(self, "dma_sems"):
            self.dma_sems = set()
        self.dma_sems.add(sem)
        return self.op(eng, lambda e: e.dma_start(
            out=out_fn(), in_=in_fn(), allow_slow_non_contiguous=slow),
            [(sem, 16)])

    def barrier(self):
        for qsem in sorted(getattr(self, "dma_sems", ())):
            c = self.cnt.get(qsem, 0)
            if c:
                self.w("y", (qsem, c))
        evs = [self.mark(k) for k in "tvsg"]
        evs.append(self.op("y", lambda e: e.nop(), [("y", 1)]))
        for k in "tvsyg":
            for e in evs:
                self.w(k, e)

    def emit(self, block, sems):
        decos = {"t": block.tensor, "v": block.vector, "s": block.scalar,
                 "y": block.sync, "g": block.gpsimd}
        for key in "tvsyg":
            ops = self.ops[key]

            def make(ops=ops):
                def body(eng):
                    hi = {}
                    for item in ops:
                        if item[0] == "wait":
                            n, val = item[1]
                            if hi.get(n, 0) >= val:
                                continue
                            eng.wait_ge(sems[n], val)
                            hi[n] = val
                        else:
                            _, fn, incs = item
                            inst = fn(eng)
                            for (n, k) in incs:
                                inst.then_inc(sems[n], k)
                return body
            decos[key](make())


def AP(t, off, pat):
    if isinstance(t, bass.AP):
        t = t.tensor
    return bass.AP(t, off, pat)


class Bank:
    def __init__(self):
        self.ready = None
        self.free = None


def build_nc(stage, cut=None):
    nc = bass.Bass("TRN2", target_bir_lowering=False, debug=False,
                   detect_race_conditions=False)
    p = Plan()

    x_hbm = nc.dram_tensor("x", [NB * SEQ, D], F32, kind="ExternalInput").ap()
    out_hbm = nc.dram_tensor("out", [NB * SEQ, D], F32, kind="ExternalOutput").ap()
    wq_h = nc.dram_tensor("L0_attn_Wq", [D, D], F32, kind="ExternalInput").ap()
    wk_h = nc.dram_tensor("L0_attn_Wk", [D, D], F32, kind="ExternalInput").ap()
    wv_h = nc.dram_tensor("L0_attn_Wv", [D, D], F32, kind="ExternalInput").ap()
    wfc_h = nc.dram_tensor("L0_ffn_W_fc", [FF, D], F32, kind="ExternalInput").ap()
    bfc_h = nc.dram_tensor("L0_ffn_b_fc", [FF], F32, kind="ExternalInput").ap()
    afc_h = nc.dram_tensor("L0_ffn_A_fc", [16, D], F32, kind="ExternalInput").ap()
    Bfc_h = nc.dram_tensor("L0_ffn_B_fc", [FF, 16], F32, kind="ExternalInput").ap()
    wpr_h = nc.dram_tensor("L0_ffn_W_proj", [D, FF], F32, kind="ExternalInput").ap()
    bpr_h = nc.dram_tensor("L0_ffn_b_proj", [D], F32, kind="ExternalInput").ap()
    apr_h = nc.dram_tensor("L0_ffn_A_proj", [16, FF], F32, kind="ExternalInput").ap()
    Bpr_h = nc.dram_tensor("L0_ffn_B_proj", [D, 16], F32, kind="ExternalInput").ap()

    wfcT = nc.dram_tensor("wfcT", [32, 128, 1024], BF16, kind="Internal").ap()
    wprT = nc.dram_tensor("wprT", [32, 128, 1024], BF16, kind="Internal").ap()
    wqT = nc.dram_tensor("wqT", [8, 128, 1024], BF16, kind="Internal").ap()
    wkT = nc.dram_tensor("wkT", [8, 128, 1024], BF16, kind="Internal").ap()
    wvT = nc.dram_tensor("wvT", [8, 128, 1024], BF16, kind="Internal").ap()

    with ExitStack() as top:
        E = top.enter_context
        identf = E(nc.sbuf_tensor("identf", [128, 128], F32))
        identb = E(nc.sbuf_tensor("identb", [128, 128], BF16))
        qones = E(nc.sbuf_tensor("qones", [4, 128], F32))
        bpr4 = E(nc.sbuf_tensor("bpr4", [4, 1024], F32))
        bfc_nat = E(nc.sbuf_tensor("bfc_nat", [32, 128], F32))
        bfc_sb = E(nc.sbuf_tensor("bfc_sb", [128, 32], F32))
        m2s = E(nc.sbuf_tensor("m2s", [128, NTILE], F32))
        i2s = E(nc.sbuf_tensor("i2s", [128, NTILE], F32))
        ysum = E(nc.sbuf_tensor("ysum", [128, NTILE], F32))
        ysq = E(nc.sbuf_tensor("ysq", [128, NTILE], F32))
        gnorm = E(nc.sbuf_tensor("gnorm", [128, 2], F32))
        sc_pp = E(nc.sbuf_tensor("sc_pp", [128, 2], F32))
        ln_tmpA = E(nc.sbuf_tensor("ln_tmpA", [128, 1024], F32))
        ln_tmpB = E(nc.sbuf_tensor("ln_tmpB", [128, 1024], F32))
        lnst = E(nc.sbuf_tensor("lnst", [128, 28], F32))
        rc_sb = E(nc.sbuf_tensor("rc_sb", [128, 2], F32))
        fold_t = E(nc.sbuf_tensor("fold_t", [128, 1024], F32))
        sqdump = fold_t
        negc_pp = E(nc.sbuf_tensor("negc_pp", [128, 4], F32))
        rl_st = E(nc.sbuf_tensor("rl_st", [128, 17], F32))
        o32c = E(nc.sbuf_tensor("o32c", [32, 128], F32))

        PS_A = E(nc.psum_tensor("PS_A", [128, 1024], F32))
        PS_B = E(nc.psum_tensor("PS_B", [128, 1024], F32))
        PS_C = E(nc.psum_tensor("PS_C", [128, 2048], F32))
        bk = {n: Bank() for n in ("A0", "A1", "B0", "B1", "C0", "C1", "C2", "C3")}
        lt_last = {0: None, 1: None}

        sems = {}
        for n in ("t", "v", "s", "g", "y", "dy", "ds", "dg", "sA0", "sA1",
                  "sB0", "sB1", "sq0", "sq1", "sq2", "sq3", "sv0", "sg0",
                  "sg1", "dsw", "dgs", "dsm", "dab", "dcn", "dbc"):
            sems[n] = E(nc.semaphore("sem_" + n))

        ev = p.g(lambda e: e.memset(identf[:], 0.0), inc=True)
        p.w("g", ev)
        ev = p.g(lambda e: e.affine_select(out=identf[:], in_=identf[:],
                                           compare_op=ALU.not_equal, fill=1.0,
                                           base=0, pattern=[[-1, 128]],
                                           channel_multiplier=1), inc=True)
        p.w("g", ev)
        ev = p.g(lambda e: e.memset(qones[:], 0.25), inc=True)
        p.w("g", ev)
        ev_cg = p.g(lambda e: e.memset(o32c[:], 1.0 / 32.0), inc=True)
        p.w("g", ev_cg)
        p.w("v", ev_cg)
        ev_identb = p.v(lambda e: e.tensor_copy(out=identb[:], in_=identf[:]), inc=True)
        p.w("t", ev_cg)
        p.w("t", ev_identb)

        ev_bn = p.dma("y", lambda: bfc_nat[:],
                      lambda: AP(bfc_h, 0, [[128, 32], [1, 128]]), sem="dcn")
        for rr in range(4):
            p.dma("y", lambda rr=rr: bpr4[rr:rr + 1, :],
                  lambda: AP(bpr_h, 0, [[1024, 1], [1, 1024]]), sem="dcn")
        p.w("t", ev_bn)
        p.w("t", bk["A0"].free)
        ev = p.t(lambda e: e.matmul(PS_A[:, 0:32], bfc_nat[:],
                                    identf[0:32, 0:32],
                                    start=True, stop=True, is_transpose=True),
                 inc=True)
        bk["A0"].ready = ev
        p.w("v", ev)
        ev_bfc = ev_bpr = p.v(lambda e: e.tensor_copy(out=bfc_sb[:],
                                                      in_=PS_A[:, 0:32]), inc=True)
        bk["A0"].free = ev_bfc
        p.w("t", ("dcn", 80))
        p.w("v", ("dcn", 80))


        def vw():
            p.w("v", p.mark("v"))

        def rowred_bcast(src_col, redop, dst_col):
            p.w("t", p.mark("v"))
            p.w("t", bk["A0"].free)
            ev = p.t(lambda e: e.transpose(PS_A[0:1, 0:128], src_col, identf[:]),
                     inc=True)
            bk["A0"].ready = ev
            p.w("v", ev)
            p.v(lambda e: e.memset(rc_sb[:, 1:2], 0.0))
            ev = p.v(lambda e, redop=redop: e.tensor_reduce(
                out=rc_sb[0:1, 1:2], in_=PS_A[0:1, 0:128], axis=AX.X, op=redop),
                inc=True)
            bk["A0"].free = ev
            p.w("v", ev)
            ev = p.v(lambda e: e.stream_shuffle(out=rc_sb[:, 0:1],
                                                in_=rc_sb[:, 1:2],
                                                mask=[0] * 32), inc=True)
            p.w("t", ev)
            p.w("t", bk["A0"].free)
            ev = p.t(lambda e: e.matmul(PS_A[:, 0:1], o32c[:], rc_sb[0:32, 0:1],
                                        start=True, stop=True), inc=True)
            bk["A0"].ready = ev
            p.w("v", ev)
            ev = p.v(lambda e, dst_col=dst_col: e.tensor_copy(out=dst_col,
                                                              in_=PS_A[:, 0:1]),
                     inc=True)
            bk["A0"].free = ev
            p.w("v", ev)
            return ev

        out_evs = []
        do_fc = stage not in ("io",)
        stage = stage if stage not in () else stage
        do_pr = stage not in ("io", "prep1", "prep1a", "prep1b")
        do_qkv = stage not in ("io", "prep1", "prep1a", "prep1b", "prep1c", "prep1d", "prep2")
        def _prepass(pre):
            Ep = pre.enter_context
            wfull = Ep(nc.sbuf_tensor("wfull", [128, 32768], F32))
            absst = Ep(nc.sbuf_tensor("absst", [128, 34], F32))
            rnd_nf = Ep(nc.sbuf_tensor("rnd_nf", [128, 2048], F32))
            rnd_d = Ep(nc.sbuf_tensor("rnd_d", [128, 2048], F32))
            rnd_i = Ep(nc.sbuf_tensor("rnd_i", [128, 2048], I32))
            stg = Ep(nc.sbuf_tensor("stg", [128, 2048], BF16))
            ab_ld = Ep(nc.sbuf_tensor("ab_ld", [16, 1024], F32))
            bt_nat = Ep(nc.sbuf_tensor("bt_nat", [128, 512], F32))
            a_bf = Ep(nc.sbuf_tensor("a_bf", [16, 4096], BF16))
            b_bf = Ep(nc.sbuf_tensor("b_bf", [16, 4096], BF16))
            stg_free = [None, None]

            def absmax_to_scpp(nchunks, chw):
                for i in range(nchunks):
                    p.v(lambda e, i=i, chw=chw: e.tensor_reduce(
                        out=absst[:, i:i + 1], in_=wfull[:, i * chw:(i + 1) * chw],
                        axis=AX.X, op=ALU.max, apply_absolute_value=True))
                vw()
                ev = p.v(lambda e, n=nchunks: e.tensor_reduce(
                    out=absst[:, 32:33], in_=absst[:, 0:n], axis=AX.X, op=ALU.max),
                    inc=True)
                p.w("v", ev)
                rowred_bcast(absst[:, 32:33], ALU.max, absst[:, 33:34])
                ev = p.v(lambda e: e.tensor_single_scalar(
                    out=sc_pp[:, 1:2], in_=absst[:, 33:34], scalar=1.0 / 127.0,
                    op=ALU.mult), inc=True)
                p.w("v", ev)
                ev = p.v(lambda e: e.reciprocal(out=sc_pp[:, 0:1],
                                                in_=sc_pp[:, 1:2]), inc=True)
                p.w("v", ev)
                p.w("s", ev)
                p.w("t", ev)

            def quantize_span(lo, w):
                src = wfull[:, lo:lo + w]
                nf, dd, ii = rnd_nf[:, 0:w], rnd_d[:, 0:w], rnd_i[:, 0:w]
                ev = p.v(lambda e, src=src: e.tensor_scalar(
                    out=src, in0=src, scalar1=sc_pp[:, 0:1], scalar2=None,
                    op0=ALU.mult), inc=True)
                p.w("s", ev)
                p.s(lambda e, src=src, ii=ii: e.copy(out=ii, in_=src))
                ev = p.s(lambda e, nf=nf, ii=ii: e.copy(out=nf, in_=ii), inc=True)
                p.w("v", ev)
                p.v(lambda e, src=src, nf=nf, dd=dd: e.tensor_sub(
                    out=dd, in0=src, in1=nf))
                p.v(lambda e, nf=nf, dd=dd: e.scalar_tensor_tensor(
                    out=nf, in0=dd, scalar=0.5, in1=nf, op0=ALU.is_gt, op1=ALU.add))
                p.v(lambda e, src=src, nf=nf, dd=dd: e.tensor_sub(
                    out=dd, in0=src, in1=nf))
                p.v(lambda e, dd=dd: e.tensor_single_scalar(
                    out=dd, in_=dd, scalar=-0.5, op=ALU.is_lt))
                p.v(lambda e, src=src, nf=nf, dd=dd: e.tensor_sub(
                    out=src, in0=nf, in1=dd))
                return p.v(lambda e, src=src: e.tensor_scalar(
                    out=src, in0=src, scalar1=sc_pp[:, 1:2], scalar2=None,
                    op0=ALU.mult), inc=True)

            def load_ab(a_h, b_h, adim, bdim, only=None):
                for lo in range(0, adim, 1024):
                    w = min(1024, adim - lo)
                    ev = p.dma("y", lambda w=w: ab_ld[:, 0:w],
                               lambda a_h=a_h, lo=lo, w=w: a_h[:, lo:lo + w],
                               sem="dab")
                    p.w("v", ev)
                    ev_a = p.v(lambda e, lo=lo, w=w: e.tensor_single_scalar(
                        out=a_bf[:, lo:lo + w], in_=ab_ld[:, 0:w], scalar=2.0,
                        op=ALU.mult), inc=True)
                    p.w("y", ev_a)
                if only == "a":
                    return
                nch = bdim // 128
                ev_bt = p.dma("y", lambda nch=nch: bt_nat[:, 0:nch * 16],
                              lambda b_h=b_h, nch=nch: AP(
                                  b_h, 0, [[16, 128], [2048, nch], [1, 16]]),
                              sem="dab")
                p.w("t", ev_bt)
                p.w("t", ev_a)
                ev_b = None
                bankmap = [("A0", PS_A, 0), ("A1", PS_A, 512),
                           ("B0", PS_B, 0), ("B1", PS_B, 512)]
                for c in range(nch):
                    bn, pst, off = bankmap[c % 4]
                    p.w("t", bk[bn].free)
                    ev = p.t(lambda e, c=c, pst=pst, off=off: e.matmul(
                        pst[0:16, off:off + 128],
                        bt_nat[:, c * 16:(c + 1) * 16], identf[:],
                        start=True, stop=True, is_transpose=True), inc=True)
                    bk[bn].ready = ev
                    if c % 2 == 1:
                        bnames = ("A0", "A1") if (c % 4) == 1 else ("B0", "B1")
                        pst2 = PS_A if (c % 4) == 1 else PS_B
                        p.w("v", bk[bnames[0]].ready)
                        p.w("v", bk[bnames[1]].ready)
                        ev_b = p.v(lambda e, c=c, pst2=pst2: e.tensor_copy(
                            out=b_bf[:, (c - 1) * 128:(c + 1) * 128],
                            in_=AP(pst2, 0, [[1024, 16], [512, 2], [1, 128]])),
                            inc=True)
                        for bn2 in bnames:
                            bk[bn2].free = ev_b
                p.w("t", ev_b)

            def trans_block(src_slices, ba_pair, pairsel, out_fn):
                slot = trans_block.cnt % 2
                trans_block.cnt += 1
                bankmap = [("A0", PS_A, 0), ("A1", PS_A, 512),
                           ("B0", PS_B, 0), ("B1", PS_B, 512)]
                for k8 in range(8):
                    bn, pst, off = bankmap[k8 % 4]
                    p.w("t", bk[bn].free)
                    src = src_slices[k8]
                    if ba_pair:
                        p.t(lambda e, pst=pst, off=off, src=src: e.matmul(
                            pst[:, off:off + 128], src, identf[:],
                            start=True, stop=False, is_transpose=True))
                        a_sl, b_sl = ba_pair[k8]
                        ev = p.t(lambda e, pst=pst, off=off, a_sl=a_sl, b_sl=b_sl:
                                 e.matmul(pst[:, off:off + 128], a_sl, b_sl,
                                          start=False, stop=True), inc=True)
                    else:
                        ev = p.t(lambda e, pst=pst, off=off, src=src: e.matmul(
                            pst[:, off:off + 128], src, identf[:],
                            start=True, stop=True, is_transpose=True), inc=True)
                    bk[bn].ready = ev
                    if k8 % 2 == 1:
                        bnames = ("A0", "A1") if (k8 % 4) == 1 else ("B0", "B1")
                        pst2 = PS_A if (k8 % 4) == 1 else PS_B
                        p.w("v", bk[bnames[0]].ready)
                        p.w("v", bk[bnames[1]].ready)
                        if k8 == 1:
                            p.w("v", stg_free[slot])
                        dst = AP(stg, slot * 1024 + (k8 - 1) * 128,
                                 [[2048, 128], [128, 2], [1, 128]])
                        ev_e = p.v(lambda e, pst2=pst2, dst=dst: e.tensor_copy(
                            out=dst, in_=AP(pst2, 0, [[1024, 128], [512, 2],
                                                      [1, 128]])), inc=True)
                        for bn2 in bnames:
                            bk[bn2].free = ev_e
                ev_last = ev_e
                p.w("y", ev_last)
                stg_free[slot] = out_fn(stg[:, slot * 1024:(slot + 1) * 1024],
                                        "sg%d" % slot)
            trans_block.cnt = 0

            if not do_fc:
                return
            ld_evs = [p.dma("y",
                            lambda q=q: AP(wfull, q * 8192,
                                           [[32768, 128], [1024, 8], [1, 1024]]),
                            lambda q=q: AP(wfc_h, q * 8 * 128 * 1024,
                                           [[1024, 128], [131072, 8], [1, 1024]]))
                      for q in range(4)]
            p.w("v", ld_evs[-1])
            absmax_to_scpp(32, 1024)
            if stage == "pdbg":
                ev = p.mark("v")
                p.w("y", ev)
                out_evs.append(p.dma("y",
                                     lambda: AP(out_hbm, 0, [[1024, 128], [1, 2]]),
                                     lambda: sc_pp[:, 0:2]))
                out_evs.append(p.dma("y",
                                     lambda: AP(out_hbm, 128 * 1024,
                                                [[1024, 128], [1, 1024]]),
                                     lambda: wfull[:, 0:1024]))
                out_evs.append(p.dma("y",
                                     lambda: AP(out_hbm, 256 * 1024,
                                                [[1024, 128], [1, 1024]]),
                                     lambda: wfull[:, 31 * 1024:32 * 1024]))
                out_evs.append(p.dma("y",
                                     lambda: AP(out_hbm, 384 * 1024,
                                                [[1024, 128], [1, 17]]),
                                     lambda: rl_st[:, 0:17]))
                out_evs.append(p.dma("y",
                                     lambda: AP(out_hbm, 400 * 1024,
                                                [[1024, 128], [1, 34]]),
                                     lambda: absst[:, 0:34]))
                return
            if stage == "prep1a":
                return
            if stage == "prep1b":
                for i in range(32):
                    quantize_span(i * 1024, 1024)
                return
            load_ab(afc_h, Bfc_h, 1024, 4096,
                    only=("a" if stage == "prep1c" else None))
            if stage in ("prep1c", "prep1d"):
                for i in range(32):
                    quantize_span(i * 1024, 1024)
                return
            for i in range(32):
                ev_q = quantize_span(i * 1024, 1024)
                p.w("t", ev_q)
                wq_t = wfull[:, i * 1024:(i + 1) * 1024]
                trans_block(
                    [wq_t[:, dc * 128:(dc + 1) * 128] for dc in range(8)],
                    [(a_bf[:, dc * 128:(dc + 1) * 128],
                      b_bf[:, i * 128:(i + 1) * 128]) for dc in range(8)],
                    i % 2,
                    lambda sap, sg, i=i: p.dma("y", lambda i=i: wfcT[i, :, :],
                                               lambda sap=sap: sap, sem=sg))
            ev_fc_done = p.mark("t")

            if not do_pr:
                return
            p.w("y", ev_fc_done)
            ld_evs = [p.dma("y",
                            lambda q=q: AP(wfull, q * 8192,
                                           [[32768, 128], [4096, 2], [1, 4096]]),
                            lambda q=q: AP(wpr_h, q * 2 * 128 * 4096,
                                           [[4096, 128], [524288, 2], [1, 4096]]))
                      for q in range(4)]
            p.w("v", ld_evs[-1])
            absmax_to_scpp(8, 4096)
            load_ab(apr_h, Bpr_h, 4096, 1024)
            if stage == "pdbg2":
                ev = p.mark("v")
                p.w("y", ev)
                out_evs.append(p.dma("y",
                                     lambda: AP(out_hbm, 0, [[1024, 128], [1, 2]]),
                                     lambda: sc_pp[:, 0:2]))
                out_evs.append(p.dma("y",
                                     lambda: AP(out_hbm, 128 * 1024,
                                                [[1024, 128], [1, 1024]]),
                                     lambda: wfull[:, 0:1024]))
                out_evs.append(p.dma("y",
                                     lambda: AP(out_hbm, 256 * 1024,
                                                [[1024, 128], [1, 1024]]),
                                     lambda: wfull[:, 31 * 1024:32 * 1024]))
                out_evs.append(p.dma("y",
                                     lambda: AP(out_hbm, 400 * 1024,
                                                [[1024, 128], [1, 34]]),
                                     lambda: absst[:, 0:34]))
                return
            qev = None
            for oc in range(8):
                quantize_span(oc * 4096, 2048)
                qev = quantize_span(oc * 4096 + 2048, 2048)
            p.w("t", qev)
            if stage == "pdbg3":
                ev = p.mark("v")
                p.w("y", ev)
                out_evs.append(p.dma("y",
                                     lambda: AP(out_hbm, 0, [[1024, 128], [1, 2]]),
                                     lambda: sc_pp[:, 0:2]))
                out_evs.append(p.dma("y",
                                     lambda: AP(out_hbm, 128 * 1024,
                                                [[1024, 128], [1, 1024]]),
                                     lambda: wfull[:, 0:1024]))
                out_evs.append(p.dma("y",
                                     lambda: AP(out_hbm, 256 * 1024,
                                                [[1024, 128], [1, 1024]]),
                                     lambda: wfull[:, 31 * 1024:32 * 1024]))
                return
            for j in range(32):
                trans_block(
                    [wfull[:, oc * 4096 + j * 128: oc * 4096 + (j + 1) * 128]
                     for oc in range(8)],
                    [(a_bf[:, j * 128:(j + 1) * 128],
                      b_bf[:, oc * 128:(oc + 1) * 128]) for oc in range(8)],
                    j % 2,
                    lambda sap, sg, j=j: p.dma("y", lambda j=j: wprT[j, :, :],
                                               lambda sap=sap: sap, sem=sg))
            ev_pr_done = p.mark("t")

            if not do_qkv:
                return
            prev_done = ev_pr_done
            for (w_h, wT, isv) in ((wq_h, wqT, False), (wk_h, wkT, False),
                                   (wv_h, wvT, True)):
                p.w("y", prev_done)
                ev_ld = p.dma("y",
                              lambda: AP(wfull, 0,
                                         [[32768, 128], [1024, 8], [1, 1024]]),
                              lambda w_h=w_h: AP(w_h, 0,
                                                 [[1024, 128], [131072, 8],
                                                  [1, 1024]]))
                p.w("t", ev_ld)
                for oc in range(8):
                    def outdma(sap, sg, wT=wT, oc=oc, isv=isv):
                        if not isv:
                            return p.dma("y", lambda wT=wT, oc=oc: wT[oc, :, :],
                                         lambda sap=sap: sap, sem=sg)
                        evl = None
                        for dc in range(8):
                            evl = p.dma("y",
                                        lambda wT=wT, oc=oc, dc=dc:
                                        wT[dc, :, oc * 128:(oc + 1) * 128],
                                        lambda sap=sap, dc=dc:
                                        sap[:, dc * 128:(dc + 1) * 128], sem=sg)
                        return evl
                    trans_block(
                        [wfull[:, oc * 1024 + dc * 128: oc * 1024 + (dc + 1) * 128]
                         for dc in range(8)],
                        None, oc % 2, outdma)
                prev_done = p.mark("t")

        with ExitStack() as pre:
            _prepass(pre)
        p.barrier()

        if stage in ("wdump", "wdump2", "wdumpq"):
            with ExitStack() as wd:
                wb = wd.enter_context(nc.sbuf_tensor("wdmp", [128, 1024], BF16))
                wf = wd.enter_context(nc.sbuf_tensor("wdmf", [128, 1024], F32))
                srcs = {"wdump": [wfcT[i, :, :] for i in range(32)],
                        "wdump2": [wprT[i, :, :] for i in range(32)],
                        "wdumpq": [t[i, :, :] for t in (wqT, wkT, wvT)
                                   for i in range(8)]}[stage]
                prev = None
                for i, src in enumerate(srcs):
                    ev = p.dma("y", lambda: wb[:], lambda src=src: src,
                               sem="dsm")
                    p.w("v", ev)
                    p.w("v", prev)
                    ev2 = p.v(lambda e: e.tensor_copy(out=wf[:], in_=wb[:]),
                              inc=True)
                    p.w("y", ev2)
                    out_evs.append(p.dma(
                        "y",
                        lambda i=i: AP(out_hbm, i * 128 * 1024,
                                       [[1024, 128], [1, 1024]]),
                        lambda: wf[:]))
                    prev = out_evs[-1]
                p.barrier()

        with ExitStack() as mainstack:
            Em = mainstack.enter_context
            xs = Em(nc.sbuf_tensor("xs", [128, NTILE * 1024], F32))
            Y = Em(nc.sbuf_tensor("Y", [128, NTILE * 1024], F32))

            def xst(t):
                return xs[:, t * 1024:(t + 1) * 1024]

            def Yt(t):
                return Y[:, t * 1024:(t + 1) * 1024]

            def ln_block(tiles, lnT, fold, save_m2):
                fold_evs = []
                for k, (tix, xsl) in enumerate(tiles):
                    fold_evs.append(fold(tix, xsl) if fold else None)
                sq_evs = []
                for k, (tix, xsl) in enumerate(tiles):
                    p.w("v", fold_evs[k])
                    p.v(lambda e, xsl=xsl, k=k: e.tensor_reduce(
                        out=lnst[:, k:k + 1], in_=xsl, axis=AX.X, op=ALU.add))
                    p.w("s", fold_evs[-1])
                    sq_evs.append(p.s(lambda e, xsl=xsl, k=k: e.activation(
                        out=sqdump[:], in_=xsl, func=AF.Square,
                        accum_out=lnst[:, 4 + k:5 + k]), inc=True))
                for ev in sq_evs:
                    p.w("v", ev)
                vw()
                p.v(lambda e: e.tensor_single_scalar(
                    out=lnst[:, 8:12], in_=lnst[:, 0:4], scalar=1.0 / 1024.0,
                    op=ALU.mult))
                p.v(lambda e: e.tensor_single_scalar(
                    out=lnst[:, 12:16], in_=lnst[:, 4:8], scalar=1.0 / 1024.0,
                    op=ALU.mult))
                vw()
                p.v(lambda e: e.tensor_mul(out=lnst[:, 16:20], in0=lnst[:, 8:12],
                                           in1=lnst[:, 8:12]))
                vw()
                p.v(lambda e: e.tensor_sub(out=lnst[:, 12:16], in0=lnst[:, 12:16],
                                           in1=lnst[:, 16:20]))
                ev = p.v(lambda e: e.tensor_single_scalar(
                    out=lnst[:, 12:16], in_=lnst[:, 12:16], scalar=EPS,
                    op=ALU.add), inc=True)
                p.w("s", ev)
                ev = p.s(lambda e: e.activation(out=lnst[:, 16:20],
                                                in_=lnst[:, 12:16], func=AF.Sqrt),
                         inc=True)
                p.w("v", ev)
                p.v(lambda e: e.reciprocal(out=lnst[:, 20:24], in_=lnst[:, 16:20]))
                vw()
                p.v(lambda e: e.tensor_mul(out=lnst[:, 20:24], in0=lnst[:, 20:24],
                                           in1=lnst[:, 12:16]))
                vw()
                p.v(lambda e: e.tensor_add(out=lnst[:, 20:24], in0=lnst[:, 20:24],
                                           in1=lnst[:, 16:20]))
                vw()
                p.v(lambda e: e.tensor_single_scalar(
                    out=lnst[:, 20:24], in_=lnst[:, 20:24], scalar=0.5,
                    op=ALU.mult))
                vw()
                p.v(lambda e: e.reciprocal(out=lnst[:, 24:28], in_=lnst[:, 20:24]))
                vw()
                t0 = tiles[0][0]
                p.v(lambda e, t0=t0: e.tensor_copy(out=m2s[:, t0:t0 + 4],
                                                   in_=lnst[:, 8:12]))
                p.v(lambda e, t0=t0: e.tensor_copy(out=i2s[:, t0:t0 + 4],
                                                   in_=lnst[:, 24:28]))
                last_evac = None
                bankmap = [("A0", PS_A, 0), ("A1", PS_A, 512),
                           ("B0", PS_B, 0), ("B1", PS_B, 512)]
                for k, (tix, xsl) in enumerate(tiles):
                    lt = ln_tmpA if k % 2 == 0 else ln_tmpB
                    p.w("v", lt_last[k % 2])
                    ev_ap = p.v(lambda e, xsl=xsl, k=k, lt=lt: e.tensor_scalar(
                        out=lt[:], in0=xsl, scalar1=lnst[:, 8 + k:9 + k],
                        scalar2=lnst[:, 24 + k:25 + k], op0=ALU.subtract,
                        op1=ALU.mult), inc=True)
                    p.w("t", ev_ap)
                    tc = tix % 4
                    for dc in range(8):
                        bn, pst, off = bankmap[dc % 4]
                        p.w("t", bk[bn].free)
                        ev = p.t(lambda e, pst=pst, off=off, dc=dc, lt=lt:
                                 e.matmul(pst[:, off:off + 128],
                                          lt[:, dc * 128:(dc + 1) * 128],
                                          identf[:], start=True, stop=True,
                                          is_transpose=True), inc=True)
                        bk[bn].ready = ev
                        if dc % 2 == 1:
                            bnames = ("A0", "A1") if (dc % 4) == 1 else ("B0", "B1")
                            pst2 = PS_A if (dc % 4) == 1 else PS_B
                            p.w("v", bk[bnames[0]].ready)
                            p.w("v", bk[bnames[1]].ready)
                            dst = AP(lnT, (dc - 1) * 512 + tc * 128,
                                     [[4096, 128], [512, 2], [1, 128]])
                            ev_e = p.v(lambda e, pst2=pst2, dst=dst: e.tensor_copy(
                                out=dst, in_=AP(pst2, 0, [[1024, 128], [512, 2],
                                                          [1, 128]])), inc=True)
                            for bn2 in bnames:
                                bk[bn2].free = ev_e
                            last_evac = ev_e
                    lt_last[k % 2] = ev
                return last_evac

            def ffn_call(b, call_id, fold, store_out):
                with ExitStack() as fs:
                    Ef = fs.enter_context
                    lnT = Ef(nc.sbuf_tensor("lnT_%d_%d" % (b, call_id),
                                            [128, 4096], BF16))
                    h_sb = Ef(nc.sbuf_tensor("h_%d_%d" % (b, call_id),
                                             [128, 16384], BF16))
                    wbufA = Ef(nc.sbuf_tensor("wbA_%d_%d" % (b, call_id),
                                              [128, 2048], BF16))
                    wbufB = Ef(nc.sbuf_tensor("wbB_%d_%d" % (b, call_id),
                                              [128, 2048], BF16))
                    fcw_free = [None, None]
                    prw_free = [None, None]
                    h_free = [None] * 32
                    for blk in range(NBLK):
                        tiles = [(blk * 4 + tc, xst(blk * 4 + tc))
                                 for tc in range(4)]
                        ln_last = ln_block(tiles, lnT, fold, False)
                        if stage == "ln":
                            if store_out:
                                for tc in range(4):
                                    tix = blk * 4 + tc
                                    out_evs.append(p.dma(
                                        "y",
                                        lambda b=b, tix=tix: AP(
                                            out_hbm, (b * SEQ + tix * 128) * 1024,
                                            [[1024, 128], [1, 1024]]),
                                        lambda tix=tix: xst(tix)))
                            continue

                        gelu_evs = []
                        for j in range(32):
                            slot = j % 2
                            p.w("y", fcw_free[slot])
                            ev_w = p.dma("y",
                                         lambda slot=slot: wbufA[:, slot * 1024:
                                                                 (slot + 1) * 1024],
                                         lambda j=j: wfcT[j, :, :],
                                         sem="sA%d" % slot)
                            cb = "C%d" % (j % 4)
                            co = (j % 4) * 512
                            p.w("t", ev_w)
                            p.w("t", ln_last)
                            p.w("t", bk[cb].free)
                            for dc in range(8):
                                last = dc == 7
                                ev = p.op("t", lambda e, co=co, dc=dc, slot=slot,
                                          last=last: e.matmul(
                                              PS_C[:, co:co + 512],
                                              wbufA[:, slot * 1024 + dc * 128:
                                                    slot * 1024 + (dc + 1) * 128],
                                              lnT[:, dc * 512:(dc + 1) * 512],
                                              start=(dc == 0), stop=last),
                                          [("t", 1)] if last else ())
                            bk[cb].ready = ev
                            fcw_free[slot] = ev
                            p.w("s", bk[cb].ready)
                            p.w("s", ev_bfc)
                            p.w("s", h_free[j])
                            ev_g = p.s(lambda e, co=co, j=j: e.activation(
                                out=h_sb[:, j * 512:(j + 1) * 512],
                                in_=PS_C[:, co:co + 512], func=AF.Gelu,
                                bias=bfc_sb[:, j:j + 1], scale=1.0), inc=True)
                            bk[cb].free = ev_g
                            gelu_evs.append(ev_g)
                        if stage == "fc":
                            if store_out:
                                for tc in range(4):
                                    tix = blk * 4 + tc
                                    out_evs.append(p.dma(
                                        "y",
                                        lambda b=b, tix=tix: AP(
                                            out_hbm, (b * SEQ + tix * 128) * 1024,
                                            [[1024, 128], [1, 1024]]),
                                        lambda tix=tix: xst(tix)))
                            continue

                        pmap = [(PS_A, 0, ("A0", "A1")), (PS_B, 0, ("B0", "B1")),
                                (PS_C, 0, ("C0", "C1")), (PS_C, 1024, ("C2", "C3"))]
                        for tc in range(4):
                            for bn in pmap[tc][2]:
                                p.w("t", bk[bn].free)
                        for j in range(32):
                            slot = j % 2
                            p.w("y", prw_free[slot])
                            ev_w = p.dma("y",
                                         lambda slot=slot: wbufB[:, slot * 1024:
                                                                 (slot + 1) * 1024],
                                         lambda j=j: wprT[j, :, :],
                                         sem="sB%d" % slot)
                            p.w("t", ev_w)
                            p.w("t", gelu_evs[j])
                            ev = None
                            for tc in range(4):
                                pt, co, _ = pmap[tc]
                                for hf in range(2):
                                    last = tc == 3 and hf == 1
                                    ev = p.op("t", lambda e, pt=pt, co=co, hf=hf,
                                              tc=tc, j=j, slot=slot: e.matmul(
                                                  pt[:, co + hf * 512:
                                                     co + (hf + 1) * 512],
                                                  h_sb[:, j * 512 + tc * 128:
                                                       j * 512 + (tc + 1) * 128],
                                                  wbufB[:, slot * 1024 + hf * 512:
                                                        slot * 1024 + (hf + 1) * 512],
                                                  start=(j == 0), stop=False),
                                              [("t", 1)] if last else ())
                            prw_free[slot] = ev
                            h_free[j] = ev
                        p.w("t", ("dcn", 80))
                        for tc in range(4):
                            pt, co, bns = pmap[tc]
                            for hf in range(2):
                                ev = p.op("t", lambda e, pt=pt, co=co, hf=hf:
                                          e.matmul(
                                              pt[:, co + hf * 512:co + (hf + 1) * 512],
                                              qones[:],
                                              bpr4[:, hf * 512:(hf + 1) * 512],
                                              start=False, stop=True), [("t", 1)])
                                bk[bns[hf]].ready = ev
                        for tc in range(4):
                            pt, co, bns = pmap[tc]
                            tix = blk * 4 + tc
                            p.w("v", bk[bns[0]].ready)
                            p.w("v", bk[bns[1]].ready)
                            ev = p.v(lambda e, pt=pt, co=co, tix=tix:
                                     e.scalar_tensor_tensor(
                                         out=xst(tix), in0=pt[:, co:co + 1024],
                                         scalar=0.5, in1=xst(tix), op0=ALU.mult,
                                         op1=ALU.add), inc=True)
                            bk[bns[0]].free = ev
                            bk[bns[1]].free = ev
                            if store_out:
                                p.w("y", ev)
                                out_evs.append(p.dma(
                                    "y",
                                    lambda b=b, tix=tix: AP(
                                        out_hbm, (b * SEQ + tix * 128) * 1024,
                                        [[1024, 128], [1, 1024]]),
                                    lambda tix=tix: xst(tix)))
                p.barrier()

            def attention_batch(b):
                with ExitStack() as ats:
                    Ea = ats.enter_context
                    lnT2 = Ea(nc.sbuf_tensor("lnT2_%d" % b, [128, 4096], BF16))
                    qall = Ea(nc.sbuf_tensor("qall_%d" % b, [128, 4096], BF16))
                    qsw = Ea(nc.sbuf_tensor("qsw_%d" % b, [64, 1024], BF16))
                    ksw = Ea(nc.sbuf_tensor("ksw_%d" % b, [64, 1024], BF16))
                    kall = Ea(nc.sbuf_tensor("kall_%d" % b, [128, 4096], BF16))
                    v_sb = Ea(nc.sbuf_tensor("v_sb_%d" % b, [128, 4160], BF16))
                    Pb = [Ea(nc.sbuf_tensor("P%d_%d" % (i, b), [128, 2048], BF16))
                          for i in range(3)]
                    O_sb = Ea(nc.sbuf_tensor("O_sb_%d" % b, [128, 1024], F32))
                    wba = Ea(nc.sbuf_tensor("wba_%d" % b, [128, 4096], BF16))

                    P_free = [None, None, None]
                    P_ready = [None, None, None]
                    v_ev = {}
                    v_freed = None
                    negc_ev = None
                    exp_last = None
                    wba_free = [None] * 4
                    wba_v_free = None
                    tr_slot_free = [None, None]
                    o_free = None
                    qsw_guard = [None]
                    pcyc = [0]

                    ev_ones = p.v(lambda e: e.memset(
                        AP(v_sb, 64, [[4160, 128], [65, 64]]), 1.0), inc=True)

                    for g in range(4):
                        tiles = [(g * 4 + tc, xst(g * 4 + tc)) for tc in range(4)]
                        ln_last = ln_block(tiles, lnT2, None, True)

                        qk_evac = None
                        for (wT, dest, sl0) in ((wqT, qall, 0), (wkT, kall, 2)):
                            for oc in range(8):
                                slot = sl0 + oc % 2
                                p.w("y", wba_free[slot])
                                ev_w = p.dma(
                                    "y",
                                    lambda slot=slot: wba[:, slot * 1024:
                                                          (slot + 1) * 1024],
                                    lambda wT=wT, oc=oc: wT[oc, :, :],
                                    sem="sq%d" % slot)
                                bn = "A0" if oc % 2 == 0 else "A1"
                                co = 0 if oc % 2 == 0 else 512
                                p.w("t", ev_w)
                                p.w("t", ln_last)
                                p.w("t", bk[bn].free)
                                for dc in range(8):
                                    last = dc == 7
                                    ev = p.op("t", lambda e, co=co, dc=dc,
                                              slot=slot, last=last: e.matmul(
                                                  PS_A[:, co:co + 512],
                                                  wba[:, slot * 1024 + dc * 128:
                                                      slot * 1024 + (dc + 1) * 128],
                                                  lnT2[:, dc * 512:(dc + 1) * 512],
                                                  start=(dc == 0), stop=last),
                                              [("t", 1)] if last else ())
                                bk[bn].ready = ev
                                wba_free[slot] = ev
                                p.w("v", bk[bn].ready)
                                ev_e = p.v(lambda e, dest=dest, oc=oc, co=co:
                                           e.tensor_copy(
                                               out=AP(dest, oc * 128,
                                                      [[4096, 128], [1024, 4],
                                                       [1, 128]]),
                                               in_=PS_A[:, co:co + 512]), inc=True)
                                bk[bn].free = ev_e
                                qk_evac = ev_e

                        for hf in range(2):
                            p.w("y", wba_v_free)
                            for q4 in range(4):
                                p.w("y", wba_free[q4])
                                wba_free[q4] = None
                            wv_evs = []
                            for dc in range(8):
                                wv_evs.append(p.dma(
                                    "y",
                                    lambda dc=dc: wba[:, dc * 512:(dc + 1) * 512],
                                    lambda dc=dc, hf=hf: wvT[dc, :,
                                                             hf * 512:
                                                             (hf + 1) * 512],
                                    sem="sv0"))
                            ev_mm_last = None
                            for tc in range(4):
                                cb = "C%d" % tc
                                co = tc * 512
                                p.w("t", bk[cb].free)
                                p.w("t", wv_evs[-1])
                                p.w("t", ln_last)
                                for dc in range(8):
                                    last = dc == 7
                                    ev = p.op("t", lambda e, co=co, dc=dc, tc=tc,
                                              last=last: e.matmul(
                                                  PS_C[:, co:co + 512],
                                                  lnT2[:, dc * 512 + tc * 128:
                                                       dc * 512 + (tc + 1) * 128],
                                                  wba[:, dc * 512:(dc + 1) * 512],
                                                  start=(dc == 0), stop=last),
                                              [("t", 1)] if last else ())
                                bk[cb].ready = ev
                                ev_mm_last = ev
                                p.w("v", bk[cb].ready)
                                p.w("v", ev_ones)
                                p.w("v", v_freed)
                                ev_e = p.v(lambda e, tc=tc, hf=hf, co=co:
                                           e.tensor_copy(
                                               out=AP(v_sb,
                                                      tc * 1040 + hf * 520,
                                                      [[4160, 128], [65, 8],
                                                       [1, 64]]),
                                               in_=PS_C[:, co:co + 512]),
                                           inc=True)
                                bk[cb].free = ev_e
                                v_ev.setdefault(tc, []).append(ev_e)
                            wba_v_free = ev_mm_last
                        for q4 in range(4):
                            wba_free[q4] = wba_v_free

                        if cut is not None and cut <= 2:
                            continue
                        for pp in range(4):
                            tix = g * 4 + pp
                            pof = pp * 1024
                            avl = None
                            p.w("s", qk_evac)
                            p.w("s", qsw_guard[0])
                            p.dma("s", lambda pof=pof: qsw[0:64, :],
                                  lambda pof=pof: qall[64:128, pof:pof + 1024],
                                  sem="dsw")
                            ev_sw = p.dma("s", lambda pof=pof: ksw[0:64, :],
                                          lambda pof=pof: kall[64:128,
                                                               pof:pof + 1024],
                                          sem="dsw")
                            boff = (pp % 2) * 256 + (pp // 2) * 512
                            bkn = "B0" if pp < 2 else "B1"
                            p.w("t", qk_evac)
                            p.w("t", bk[bkn].free)
                            p.w("t", ev_sw)
                            for c1 in range(16):
                                qt, kt, qb2 = ((qall, kall, pof) if c1 % 2 == 0
                                               else (qsw, ksw, 0))
                                p.op("t", lambda e, qt=qt, kt=kt, qb2=qb2,
                                     c1=c1, boff=boff: e.matmul(
                                         PS_B[:, boff + c1 * 16:
                                              boff + (c1 + 1) * 16],
                                         qt[0:64, qb2 + (c1 // 2) * 128:
                                            qb2 + (c1 // 2 + 1) * 128],
                                         kt[0:64, qb2:qb2 + 16],
                                         start=True, stop=True), ())
                            ev_sub = p.mark("t")
                            bk[bkn].ready = ev_sub
                            p.w("v", ev_sub)
                            p.w("v", exp_last)
                            for c1 in range(16):
                                p.v(lambda e, boff=boff, c1=c1: e.tensor_reduce(
                                    out=rl_st[:, c1:c1 + 1],
                                    in_=PS_B[:, boff + c1 * 16:
                                             boff + (c1 + 1) * 16],
                                    axis=AX.X, op=ALU.max))
                            vw()
                            ev = p.v(lambda e: e.tensor_reduce(
                                out=rl_st[:, 16:17], in_=rl_st[:, 0:16],
                                axis=AX.X, op=ALU.min), inc=True)
                            p.w("v", ev)
                            bk[bkn].free = ev
                            rowred_bcast(rl_st[:, 16:17], ALU.min,
                                         rl_st[:, 16:17])
                            ev = p.v(lambda e, pp=pp: e.tensor_scalar(
                                out=negc_pp[:, pp:pp + 1], in0=rl_st[:, 16:17],
                                scalar1=-1.0, scalar2=-RLOW_OFF, op0=ALU.mult,
                                op1=ALU.add), inc=True)
                            p.w("v", ev)
                            negc_ev = ev
                            if cut is not None and cut == 12:
                                continue
                            for jc in range(8):
                                pa, po = [(0, 1), (2, 0), (1, 2)][pcyc[0] % 3]
                                pcyc[0] += 1
                                P_e, P_o = Pb[pa], Pb[po]
                                for (hq, qt, qb) in ((0, qall, pof), (1, qsw, 0)):
                                    for bn in ("A0", "A1", "B0", "B1"):
                                        p.w("t", bk[bn].free)
                                    for (kt, kb2, ps) in ((kall, pof, PS_A),
                                                          (ksw, 0, PS_B)):
                                        for half2 in range(2):
                                            last = half2 == 1
                                            ev = p.op("t", lambda e, kt=kt,
                                                      kb2=kb2, ps=ps, qt=qt,
                                                      qb=qb, half2=half2,
                                                      jc=jc: e.matmul(
                                                          ps[:, half2 * 512:
                                                             (half2 + 1) * 512],
                                                          kt[0:64,
                                                             kb2 + jc * 128:
                                                             kb2 + (jc + 1) * 128],
                                                          qt[0:64,
                                                             qb + half2 * 512:
                                                             qb + (half2 + 1) * 512],
                                                          start=True, stop=True),
                                                      [("t", 1)] if last else ())
                                            if last:
                                                (bk["A0"] if ps is PS_A
                                                 else bk["B0"]).ready = ev
                                    ev_sc = ev
                                    p.w("s", ev_sc)
                                    if hq == 0:
                                        p.w("s", negc_ev)
                                        p.w("s", P_free[pa])
                                        p.w("s", P_free[po])
                                    co = hq * 1024
                                    ev_e1 = p.s(lambda e, P_e=P_e, pp=pp, co=co:
                                                e.activation(
                                                    out=P_e[:, co:co + 1024],
                                                    in_=PS_A[:, 0:1024],
                                                    func=AF.Exp,
                                                    bias=negc_pp[:, pp:pp + 1]),
                                                inc=True)
                                    ev_e2 = p.s(lambda e, P_o=P_o, pp=pp, co=co:
                                                e.activation(
                                                    out=P_o[:, co:co + 1024],
                                                    in_=PS_B[:, 0:1024],
                                                    func=AF.Exp,
                                                    bias=negc_pp[:, pp:pp + 1]),
                                                inc=True)
                                    bk["A0"].free = ev_e1
                                    bk["A1"].free = ev_e1
                                    bk["B0"].free = ev_e2
                                    bk["B1"].free = ev_e2
                                P_ready[pa] = ev_e1
                                P_ready[po] = ev_e2
                                exp_last = ev_e2
                                for (pb_i, c2) in (() if (cut is not None and
                                                          cut <= 3) else
                                                   ((pa, 2 * jc), (po, 2 * jc + 1))):
                                    p.w("t", P_ready[pb_i])
                                    if c2 == 0:
                                        for tcq in range(4):
                                            p.w("t", bk["C%d" % tcq].free)
                                        p.w("t", o_free)
                                        for evv in v_ev.get(pp, []):
                                            p.w("t", evv)
                                    for qc in range(4):
                                        last = qc == 3
                                        ev = p.op("t", lambda e, pb_i=pb_i, c2=c2,
                                                  qc=qc, pp=pp: e.matmul(
                                                      PS_C[0:65, qc * 512:
                                                           (qc + 1) * 512],
                                                      v_sb[:, pp * 1040 + c2 * 65:
                                                           pp * 1040 +
                                                           (c2 + 1) * 65],
                                                      Pb[pb_i][:, qc * 512:
                                                               (qc + 1) * 512],
                                                      start=(c2 == 0),
                                                      stop=(c2 == 15)),
                                                  [("t", 1)] if last else ())
                                    P_free[pb_i] = ev
                                    avl = ev
                            if cut is not None and cut <= 4:
                                continue
                            v_freed = avl
                            qsw_guard[0] = avl
                            for tcq in range(4):
                                bk["C%d" % tcq].ready = avl

                            fold_last = None
                            for hq in range(2):
                                p.w("v", avl)
                                p.w("v", tr_slot_free[0])
                                p.w("v", tr_slot_free[1])
                                ev_o = p.v(lambda e, hq=hq: e.tensor_copy(
                                    out=O_sb[:, :],
                                    in_=PS_C[:, hq * 1024:(hq + 1) * 1024]),
                                    inc=True)
                                if hq == 1:
                                    for tcq in range(4):
                                        bk["C%d" % tcq].free = ev_o
                                o_free = ev_o
                                for j2 in range(8):
                                    c1 = 2 * j2 + hq
                                    sslot = j2 % 2
                                    soff = sslot * 512
                                    p.w("t", ev_o)
                                    p.w("t", tr_slot_free[sslot])
                                    p.w("t", bk["A0" if sslot == 0 else "A1"].free)
                                    ev_tr = p.t(lambda e, j2=j2, soff=soff:
                                                e.matmul(
                                                    PS_A[:, soff:soff + 65],
                                                    O_sb[0:65,
                                                         j2 * 128:(j2 + 1) * 128],
                                                    identf[0:65, 0:65],
                                                    start=True, stop=True,
                                                    is_transpose=True), inc=True)
                                    p.w("v", ev_tr)
                                    p.v(lambda e, sslot=sslot, soff=soff:
                                        e.reciprocal(
                                            out=rc_sb[:, sslot:sslot + 1],
                                            in_=PS_A[:, soff + 64:soff + 65]))
                                    ysl = Y[:, tix * 1024 + c1 * 64:
                                            tix * 1024 + (c1 + 1) * 64]
                                    xsl = xs[:, tix * 1024 + c1 * 64:
                                             tix * 1024 + (c1 + 1) * 64]
                                    p.v(lambda e, ysl=ysl, xsl=xsl, tix=tix:
                                        e.tensor_scalar(
                                            out=ysl, in0=xsl,
                                            scalar1=m2s[:, tix:tix + 1],
                                            scalar2=i2s[:, tix:tix + 1],
                                            op0=ALU.subtract, op1=ALU.mult))
                                    ev_f = p.v(lambda e, ysl=ysl, sslot=sslot,
                                               soff=soff: e.scalar_tensor_tensor(
                                                   out=ysl,
                                                   in0=PS_A[:, soff:soff + 64],
                                                   scalar=rc_sb[:, sslot:sslot + 1],
                                                   in1=ysl, op0=ALU.mult,
                                                   op1=ALU.add), inc=True)
                                    tr_slot_free[sslot] = ev_f
                                    if sslot == 0:
                                        bk["A0"].free = ev_f
                                    else:
                                        bk["A1"].free = ev_f
                                    fold_last = ev_f
                            p.w("v", fold_last)
                            p.v(lambda e, tix=tix: e.tensor_reduce(
                                out=ysum[:, tix:tix + 1], in_=Yt(tix), axis=AX.X,
                                op=ALU.add))
                            p.w("s", fold_last)
                            p.s(lambda e, tix=tix: e.activation(
                                out=sqdump[:], in_=Yt(tix), func=AF.Square,
                                accum_out=ysq[:, tix:tix + 1]))

                    ev_ss = p.mark("s")
                    p.w("v", ev_ss)
                    vw()
                    p.v(lambda e: e.tensor_reduce(out=rl_st[:, 0:1],
                                                  in_=ysum[:, :], axis=AX.X,
                                                  op=ALU.add))
                    ev = p.v(lambda e: e.tensor_reduce(out=rl_st[:, 1:2],
                                                       in_=ysq[:, :], axis=AX.X,
                                                       op=ALU.add), inc=True)
                    p.w("v", ev)
                    rowred_bcast(rl_st[:, 0:1], ALU.add, rl_st[:, 2:3])
                    rowred_bcast(rl_st[:, 1:2], ALU.add, rl_st[:, 3:4])
                    p.v(lambda e: e.tensor_single_scalar(
                        out=gnorm[:, 0:1], in_=rl_st[:, 2:3],
                        scalar=1.0 / NTOT, op=ALU.mult))
                    p.v(lambda e: e.tensor_single_scalar(
                        out=rl_st[:, 4:5], in_=rl_st[:, 3:4],
                        scalar=1.0 / NTOT, op=ALU.mult))
                    vw()
                    p.v(lambda e: e.tensor_mul(out=rl_st[:, 5:6],
                                               in0=gnorm[:, 0:1],
                                               in1=gnorm[:, 0:1]))
                    vw()
                    p.v(lambda e: e.tensor_sub(out=rl_st[:, 6:7],
                                               in0=rl_st[:, 4:5],
                                               in1=rl_st[:, 5:6]))
                    vw()
                    ev = p.v(lambda e: e.tensor_single_scalar(
                        out=rl_st[:, 6:7], in_=rl_st[:, 6:7], scalar=EPS,
                        op=ALU.add), inc=True)
                    p.w("v", ev)
                    p.w("s", ev)
                    ev = p.s(lambda e: e.activation(out=rl_st[:, 7:8],
                                                    in_=rl_st[:, 6:7],
                                                    func=AF.Sqrt), inc=True)
                    p.w("v", ev)
                    p.v(lambda e: e.reciprocal(out=rl_st[:, 8:9],
                                               in_=rl_st[:, 7:8]))
                    vw()
                    p.v(lambda e: e.tensor_mul(out=rl_st[:, 8:9],
                                               in0=rl_st[:, 8:9],
                                               in1=rl_st[:, 6:7]))
                    vw()
                    p.v(lambda e: e.tensor_add(out=rl_st[:, 8:9],
                                               in0=rl_st[:, 8:9],
                                               in1=rl_st[:, 7:8]))
                    vw()
                    p.v(lambda e: e.tensor_single_scalar(
                        out=rl_st[:, 8:9], in_=rl_st[:, 8:9], scalar=0.5,
                        op=ALU.mult))
                    vw()
                    ev = p.v(lambda e: e.reciprocal(out=gnorm[:, 1:2],
                                                    in_=rl_st[:, 8:9]), inc=True)
                    p.w("v", ev)
                p.barrier()

            for b in ([] if stage.startswith("wdump") else range(NB)):
                for blk in range(NBLK):
                    ev = p.dma("y",
                               lambda blk=blk: AP(xs, blk * 4096,
                                                  [[16384, 128], [1024, 4],
                                                   [1, 1024]]),
                               lambda b=b, blk=blk: AP(
                                   x_hbm, (b * SEQ + blk * 512) * 1024,
                                   [[1024, 128], [131072, 4], [1, 1024]]))
                p.barrier()

                if stage.startswith("pdbg"):
                    continue
                if stage in ("io", "prep1", "prep1a", "prep1b", "prep1c", "prep1d", "prep2", "prep"):
                    for tix in range(NTILE):
                        out_evs.append(p.dma(
                            "y",
                            lambda b=b, tix=tix: AP(
                                out_hbm, (b * SEQ + tix * 128) * 1024,
                                [[1024, 128], [1, 1024]]),
                            lambda tix=tix: xst(tix)))
                    p.barrier()
                    continue
                ffn_call(b, 0, None, store_out=(stage in ("ffn1", "ln", "fc")))
                if stage in ("ffn1", "ln", "fc"):
                    continue

                attention_batch(b)

                def fold_fn(tix, xsl):
                    p.v(lambda e, tix=tix: e.tensor_scalar(
                        out=fold_t[:], in0=Yt(tix), scalar1=gnorm[:, 0:1],
                        scalar2=gnorm[:, 1:2], op0=ALU.subtract, op1=ALU.mult))
                    return p.v(lambda e, xsl=xsl: e.tensor_add(
                        out=xsl, in0=xsl, in1=fold_t[:]), inc=True)

                if stage == "attn":
                    for tix in range(NTILE):
                        ev = fold_fn(tix, xst(tix))
                        p.w("y", ev)
                        out_evs.append(p.dma(
                            "y",
                            lambda b=b, tix=tix: AP(
                                out_hbm, (b * SEQ + tix * 128) * 1024,
                                [[1024, 128], [1, 1024]]),
                            lambda tix=tix: xst(tix)))
                    p.barrier()
                    continue

                ffn_call(b, 1, fold_fn, store_out=True)

        for qsem in sorted(getattr(p, "dma_sems", ())):
            p.w("y", (qsem, p.cnt.get(qsem, 0)))
        block = E(nc.Block())
        p.emit(block, sems)
    return nc


def kernel(**inputs):
    x = np.ascontiguousarray(np.asarray(inputs["L0_attn_x"], dtype=np.float32))
    weights = {k: np.ascontiguousarray(np.asarray(v, dtype=np.float32))
               for k, v in inputs.items() if k != "L0_attn_x"}
    nc = build_nc(_STAGE, _CUT)
    in_maps = []
    for c in range(8):
        m = {"x": x[2 * c:2 * c + 2].reshape(NB * SEQ, D)}
        m.update(weights)
        in_maps.append(m)
    res = run_bass_kernel_spmd(nc, in_maps, core_ids=list(range(8)))
    out = np.empty((16, SEQ, D), dtype=np.float32)
    for c in range(8):
        out[2 * c:2 * c + 2] = res.results[c]["out"].reshape(NB, SEQ, D)
    return out
